# version 02
# baseline (speedup 1.0000x reference)
import contextlib
import numpy as np
import ml_dtypes
import concourse.bass as bass
import concourse.mybir as mybir
from concourse.bass_utils import run_bass_kernel_spmd

F32 = mybir.dt.float32
BF16 = mybir.dt.bfloat16
I32 = mybir.dt.int32
AF = mybir.ActivationFunctionType
ALU = mybir.AluOpType
AXX = mybir.AxisListType.X
BF16NP = ml_dtypes.bfloat16

D = 2048
NDT = D // 128
EPS = 1e-5
HID = 5632
NHT = HID // 128

ENGINES = ("sync", "vector", "scalar", "tensor", "gpsimd")


class Sem:
    __slots__ = ("h", "n", "name")

    def __init__(self, h, name):
        self.h = h
        self.n = 0
        self.name = name


class Build:
    def __init__(self, nc, es):
        self.nc = nc
        self.es = es
        self.ops = {e: [] for e in ENGINES}
        self.all_sems = []
        self.last_inst = {e: None for e in ENGINES}

    def sem(self, name):
        s = Sem(self.es.enter_context(self.nc.semaphore(name)), name)
        self.all_sems.append(s)
        return s

    def wait(self, eng, sem, val=None):
        v = sem.n if val is None else val
        if v <= 0:
            return
        self.ops[eng].append(lambda e, s=sem.h, vv=v: e.wait_ge(s, vv))

    def do(self, eng, fn, inc=None, k=1, drain=None):
        incs = [] if inc is None else (inc if isinstance(inc, list) else [inc])
        incs = [s for s in incs if s is not None]
        for s in incs:
            s.n += k
        if drain is None:
            drain = eng in ("vector", "scalar", "gpsimd")

        def wrapped(e, _f=fn, _incs=tuple(incs), _k=k, _dr=drain):
            inst = _f(e)
            for s in _incs:
                inst = inst.then_inc(s.h, _k)
            if _dr:
                e.drain()
            return inst
        self.ops[eng].append(wrapped)

    def dma(self, eng, dst, src, sem):
        self.do(eng, lambda e, d=dst, s=src: e.dma_start(out=d, in_=s), inc=sem, k=16,
                drain=False)

    def barrier(self, sems, engines=ENGINES):
        for eng in engines:
            for s in sems:
                self.wait(eng, s)

    def finish(self):
        block = self.es.enter_context(self.nc.Block())
        for eng in ENGINES:
            ops = self.ops[eng]
            if not ops:
                continue

            def section(e, _ops=tuple(ops)):
                for fn in _ops:
                    fn(e)
            getattr(block, eng)(section)


def sbuf(nc, es, name, shape, dtype):
    return es.enter_context(nc.sbuf_tensor(name, list(shape), dtype))


def psum(nc, es, name, shape, dtype=F32):
    return es.enter_context(nc.psum_tensor(name, list(shape), dtype))


def emit_ln(b, ctx, xin_rows, xnf, ntok, S, pool, ptr, xlo=None):
    ntt = ntok // 128
    xt = pool[0:2]
    sq = pool[2]
    xn32 = pool[3] if xlo is not None else None
    st_ = ctx["ln_stats"]
    rs, rq, m, m2, var, sdv, rst, xnt = (
        st_["rs"], st_["rq"], st_["m"], st_["m2"], st_["var"], st_["sdv"], st_["rst"], st_["xnt"])
    idn = ctx["idn"]
    lx, vr, sv, sqd, xnd, pt, pev = (S["lx"], S["vr"], S["sv"], S["sqd"], S["xnd"], S["pt"], S["pev"])
    xnf_r = xnf[:].rearrange("p (dt tok) -> p dt tok", dt=NDT)
    ti0 = ctx["ln_tiles_done"]

    for ti in range(ntt):
        g = ti0 + ti
        p = ti % 2
        if ti >= 2:
            b.wait("sync", sqd, g - 1)
            b.wait("sync", xnd, g - 1)
        b.dma("sync", xt[p][:], xin_rows[ti * 128:(ti + 1) * 128, :], lx[p])
        b.wait("vector", lx[p], None)
        b.do("vector", lambda e, x=xt[p]: e.reduce_sum(rs[:], x[:], axis=AXX), inc=vr)
        b.wait("scalar", lx[p], None)
        b.wait("scalar", vr, None)
        b.do("scalar", lambda e, x=xt[p]: e.activation(sq[:], x[:], AF.Square), inc=sqd)
        b.wait("vector", sqd, None)
        b.do("vector", lambda e: e.reduce_sum(rq[:], sq[:], axis=AXX), inc=vr)
        b.wait("vector", vr, None)
        b.do("vector", lambda e: e.tensor_scalar(m[:], rs[:], 1.0 / D, None, ALU.mult))
        b.do("vector", lambda e: e.tensor_tensor(m2[:], m[:], m[:], ALU.mult))
        vs = var[:, p:p + 1]
        b.do("vector", lambda e, vv=vs: e.tensor_scalar(vv, rq[:], 1.0 / D, None, ALU.mult))
        if g >= 2:
            b.wait("vector", sv, g - 1)
        b.do("vector", lambda e, vv=vs: e.tensor_tensor(vv, vv, m2[:], ALU.subtract))
        b.do("vector", lambda e, vv=vs: e.tensor_scalar(vv, vv, EPS, None, ALU.add), inc=vr)
        b.wait("scalar", vr, None)
        b.do("scalar", lambda e, vv=vs, dd=sdv[:, p:p + 1]: e.activation(dd, vv, AF.Sqrt), inc=sv)
        b.wait("vector", sv, None)
        b.do("vector", lambda e, dd=sdv[:, p:p + 1]: e.reciprocal(rst[:], dd), inc=vr)
        b.wait("vector", vr, None)
        if xlo is None:
            b.do("vector", lambda e, x=xt[p]: e.tensor_scalar(
                xnt[:], x[:], m[:], rst[:], ALU.subtract, ALU.mult), inc=xnd)
        else:
            xlot = ctx["ln_stats"]["xlot"]
            b.do("vector", lambda e, x=xt[p]: e.tensor_scalar(
                xn32[:], x[:], m[:], rst[:], ALU.subtract, ALU.mult))
            b.do("vector", lambda e: e.tensor_copy(xnt[:], xn32[:]))
            b.do("vector", lambda e: e.tensor_tensor(xlot[:], xn32[:], xnt[:], ALU.subtract),
                 inc=xnd)
        b.wait("tensor", xnd, None)
        xnf_pairs = [(xnf, xnt)] if xlo is None else [(xnf, xnt), (xlo, xlot)]
        nbk = ctx["ln_banks_done"]
        for dstbuf, srct in xnf_pairs:
            dst_r = dstbuf[:].rearrange("p (dt tok) -> p dt tok", dt=NDT)
            for db in range(2):
                bank = ptr[nbk % 2]
                if nbk >= 2:
                    b.wait("tensor", pev, nbk - 1)
                for k in range(8):
                    dt = db * 8 + k
                    b.do("tensor", lambda e, bk=bank, kk=k, d=dt, s=srct: e.transpose(
                        bk[:, kk * 128:(kk + 1) * 128], s[:, d * 128:(d + 1) * 128], idn[:]),
                        inc=(pt if k == 7 else None))
                b.wait("scalar", pt, nbk + 1)
                dst = dst_r[:, db * 8:(db + 1) * 8, ti * 128:(ti + 1) * 128]
                srcb = bank[:].rearrange("p (a f) -> p a f", a=8)
                b.do("scalar", lambda e, dd=dst, s=srcb: e.copy(dd, s), inc=pev)
                nbk += 1
        ctx["ln_banks_done"] = nbk
    ctx["ln_tiles_done"] += ntt


def make_ln_ctx(b, ctx):
    nc, es = b.nc, b.es
    ctx["ln_stats"] = dict(
        rs=sbuf(nc, es, "ln_rs", (128, 1), F32),
        rq=sbuf(nc, es, "ln_rq", (128, 1), F32),
        m=sbuf(nc, es, "ln_m", (128, 1), F32),
        m2=sbuf(nc, es, "ln_m2", (128, 1), F32),
        var=sbuf(nc, es, "ln_var", (128, 2), F32),
        sdv=sbuf(nc, es, "ln_sdv", (128, 2), F32),
        rst=sbuf(nc, es, "ln_rst", (128, 1), F32),
        xnt=sbuf(nc, es, "ln_xnt", (128, D), BF16),
        xlot=sbuf(nc, es, "ln_xlot", (128, D), BF16),
    )
    ctx["ln_tiles_done"] = 0
    ctx["ln_banks_done"] = 0


def ln_sems(b, tag):
    return dict(lx=[b.sem(f"{tag}lx0"), b.sem(f"{tag}lx1")], vr=b.sem(f"{tag}vr"),
                sv=b.sem(f"{tag}sv"), sqd=b.sem(f"{tag}sqd"), xnd=b.sem(f"{tag}xnd"),
                pt=b.sem(f"{tag}pt"), pev=b.sem(f"{tag}pev"))


def build_l0ffn(T=2048):
    nc = bass.Bass("TRN2", target_bir_lowering=False)
    xin = nc.dram_tensor("x", [T, D], F32, kind="ExternalInput").ap()
    wgT = nc.dram_tensor("wgT", [D, HID], I32, kind="ExternalInput").ap()
    wuT = nc.dram_tensor("wuT", [D, HID], I32, kind="ExternalInput").ap()
    wdT = nc.dram_tensor("wdT", [HID, D], I32, kind="ExternalInput").ap()
    sg = nc.dram_tensor("sg", [128, NHT], F32, kind="ExternalInput").ap()
    su = nc.dram_tensor("su", [128, NHT], F32, kind="ExternalInput").ap()
    sd = nc.dram_tensor("sd", [128, NDT], F32, kind="ExternalInput").ap()
    idnp = nc.dram_tensor("idn", [128, 128], BF16, kind="ExternalInput").ap()
    yout = nc.dram_tensor("y", [T, D], F32, kind="ExternalOutput").ap()

    es = contextlib.ExitStack()
    with es:
        b = Build(nc, es)
        ctx = {}
        ctx["idn"] = sbuf(nc, es, "idn_sb", (128, 128), BF16)
        sgs = sbuf(nc, es, "sgs", (128, NHT), F32)
        sus = sbuf(nc, es, "sus", (128, NHT), F32)
        sds = sbuf(nc, es, "sds", (128, NDT), F32)
        make_ln_ctx(b, ctx)
        lc = b.sem("lc")
        b.dma("sync", ctx["idn"][:], idnp, lc)
        b.dma("sync", sgs[:], sg, lc)
        b.dma("sync", sus[:], su, lc)
        b.dma("sync", sds[:], sd, lc)
        b.barrier([lc])
        emit_l0ffn_body(b, ctx, xin, xin, yout, wgT, wuT, wdT, sgs, sus, sds, T,
                        resid_scale=0.5, tag="f")
        b.finish()
    return nc


def emit_l0ffn_body(b, ctx, xin, resin, yout, wgT, wuT, wdT, sgs, sus, sds, T,
                    resid_scale, tag):
    nc, es = b.nc, b.es
    TC = 1024
    nchunk = T // TC
    HS = 128
    nhs = HID // HS
    hps = HS // 128

    with contextlib.ExitStack() as ph:
        idn = ctx["idn"]
        xnf = sbuf(nc, ph, f"{tag}_xnf", (128, NDT * TC), BF16)
        gbuf = sbuf(nc, ph, f"{tag}_g", (128, NHT * TC), BF16)
        wg = [sbuf(nc, ph, f"{tag}_wg{i}", (128, NDT * HS), BF16) for i in range(2)]
        wu = [sbuf(nc, ph, f"{tag}_wu{i}", (128, NDT * HS), BF16) for i in range(2)]
        wd = [sbuf(nc, ph, f"{tag}_wd{i}", (128, 768), BF16) for i in range(2)]
        tsil = sbuf(nc, ph, f"{tag}_tsil", (128, 512), BF16)
        tup = sbuf(nc, ph, f"{tag}_tup", (128, 512), BF16)
        zbuf = sbuf(nc, ph, f"{tag}_z", (128, 6 * 512), BF16)
        ztm = sbuf(nc, ph, f"{tag}_ztm", (128, 4 * D), BF16)
        pool = [sbuf(nc, ph, f"{tag}_pool{i}", (128, D), F32) for i in range(4)]

        ptr = [psum(nc, ph, f"{tag}_ptr{i}", (128, 1024), BF16) for i in range(2)]
        pg = [psum(nc, ph, f"{tag}_pg{i}", (128, 512), F32) for i in range(2)]
        pu = [psum(nc, ph, f"{tag}_pu{i}", (128, 512), F32) for i in range(2)]
        pzb = [psum(nc, ph, f"{tag}_pz{i}", (128, 512), F32) for i in range(2)]

        wgT_r = wgT.rearrange("(dt p) h -> p dt h", p=128)
        wuT_r = wuT.rearrange("(dt p) h -> p dt h", p=128)
        wdT_r = wdT.rearrange("(ht p) c -> p ht c", p=128)
        gbuf_r = gbuf[:].rearrange("p (ht tok) -> p ht tok", ht=NHT)
        zbuf_r = zbuf[:].rearrange("p (dd tok) -> p dd tok", dd=6)
        ztm_r = ztm[:].rearrange("p (q d) -> p q d", q=4)

        SL = ln_sems(b, tag)
        lw = b.sem(f"{tag}_lw")
        ws = b.sem(f"{tag}_ws")
        mmp = b.sem(f"{tag}_mmp")
        gph = b.sem(f"{tag}_gph")
        sgph = b.sem(f"{tag}_sgph")
        ld = [b.sem(f"{tag}_ld0"), b.sem(f"{tag}_ld1")]
        wds = b.sem(f"{tag}_wds")
        mslab = b.sem(f"{tag}_mslab")
        zev = b.sem(f"{tag}_zev")
        lr = [b.sem(f"{tag}_lr0"), b.sem(f"{tag}_lr1")]
        ptz = b.sem(f"{tag}_ptz")
        pzev = b.sem(f"{tag}_pzev")
        adn = b.sem(f"{tag}_adn")
        st = [b.sem(f"{tag}_st0"), b.sem(f"{tag}_st1")]

        grp_banks = [pg[0], pg[1], pu[0], pu[1], pzb[0], pzb[1]]
        GRPS = [(0, 6), (6, 12), (12, 16)]

        st_snap = [0, 0]
        nuse = 0
        nslab = 0
        ngrp = 0
        nbank2 = 0
        zev_hist = [0]

        for c in range(nchunk):
            b.barrier([mmp, adn, st[0], st[1], zev, ptz, pzev, mslab])
            emit_ln(b, ctx, xin[c * TC:(c + 1) * TC, :], xnf, TC, SL, pool, ptr)
            b.barrier([SL["pev"], SL["xnd"], SL["sqd"]])

            for hs in range(nhs):
                pbuf = hs % 2
                if hs == 0:
                    for wbuf, src in ((wg[0], wgT_r), (wu[0], wuT_r)):
                        for q in range(4):
                            b.dma("gpsimd",
                                  wbuf[:].rearrange("p (dt h) -> p dt h", dt=NDT)[:, q * 4:(q + 1) * 4, :],
                                  src[:, q * 4:(q + 1) * 4, 0:HS], lw)
                    b.wait("gpsimd", lw, None)
                    b.do("gpsimd", lambda e, w=wg[0]: e.tensor_scalar(w[:], w[:], -128.0, None, ALU.add), inc=ws)
                    b.do("gpsimd", lambda e, w=wu[0]: e.tensor_scalar(w[:], w[:], -128.0, None, ALU.add), inc=ws)
                if hs + 1 < nhs:
                    nbuf = (hs + 1) % 2
                    if hs >= 1:
                        b.wait("gpsimd", mmp, 2 * (c * nhs + hs))
                    for wbuf, src in ((wg[nbuf], wgT_r), (wu[nbuf], wuT_r)):
                        for q in range(4):
                            b.dma("gpsimd",
                                  wbuf[:].rearrange("p (dt h) -> p dt h", dt=NDT)[:, q * 4:(q + 1) * 4, :],
                                  src[:, q * 4:(q + 1) * 4, (hs + 1) * HS:(hs + 2) * HS], lw)
                    b.wait("gpsimd", lw, None)
                    b.do("gpsimd", lambda e, w=wg[nbuf]: e.tensor_scalar(w[:], w[:], -128.0, None, ALU.add), inc=ws)
                    b.do("gpsimd", lambda e, w=wu[nbuf]: e.tensor_scalar(w[:], w[:], -128.0, None, ALU.add), inc=ws)
                b.wait("tensor", ws, 2 * (c * nhs + hs + 1))
                wgr = wg[pbuf][:].rearrange("p (dt h) -> p dt h", dt=NDT)
                wur = wu[pbuf][:].rearrange("p (dt h) -> p dt h", dt=NDT)
                for t5 in range(2):
                    for hh in range(hps):
                        ht = hs * hps + hh
                        pgb, pub = pg[nuse % 2], pu[nuse % 2]
                        if nuse >= 2:
                            b.wait("tensor", gph, nuse - 1)
                            b.wait("tensor", sgph, nuse - 1)
                        for dt in range(NDT):
                            b.do("tensor", lambda e, P=pgb, W=wgr, d=dt, h=hh, t=t5: e.matmul(
                                P[:], W[:, d, h * 128:(h + 1) * 128],
                                xnf[:, d * TC + t * 512: d * TC + t * 512 + 512],
                                start=(d == 0), stop=(d == NDT - 1)))
                        for dt in range(NDT):
                            b.do("tensor", lambda e, P=pub, W=wur, d=dt, h=hh, t=t5: e.matmul(
                                P[:], W[:, d, h * 128:(h + 1) * 128],
                                xnf[:, d * TC + t * 512: d * TC + t * 512 + 512],
                                start=(d == 0), stop=(d == NDT - 1)),
                                inc=(mmp if dt == NDT - 1 else None))
                        b.wait("scalar", mmp, None)
                        b.wait("scalar", gph, nuse)
                        b.do("scalar", lambda e, P=pgb, h=ht: e.activation(
                            tsil[:], P[:], AF.Silu, scale=sgs[:, h:h + 1]), inc=sgph)
                        b.wait("vector", sgph, None)
                        b.do("vector", lambda e, P=pub, h=ht: e.tensor_scalar(
                            tup[:], P[:], sus[:, h:h + 1], None, ALU.mult))
                        b.do("vector", lambda e, h=ht, t=t5: e.tensor_tensor(
                            gbuf_r[:, h, t * 512:(t + 1) * 512], tsil[:], tup[:], ALU.mult), inc=gph)
                        nuse += 1

            b.barrier([gph, sgph, mmp])
            for t5 in range(2):
                adn_base = adn.n
                for gi, (d0, d1) in enumerate(GRPS):
                    nd = d1 - d0
                    for ht in range(NHT):
                        pbuf = nslab % 2
                        if nslab >= 2:
                            b.wait("gpsimd", mslab, nslab - 1)
                        b.dma("gpsimd", wd[pbuf][:, 0:nd * 128],
                              wdT_r[:, ht, d0 * 128:d1 * 128], ld[pbuf])
                        b.wait("vector", ld[pbuf], None)
                        b.wait("vector", mslab, max(0, nslab - 1))
                        b.do("vector", lambda e, w=wd[pbuf], n=nd * 128: e.tensor_scalar(
                            w[:, 0:n], w[:, 0:n], -128.0, None, ALU.add), inc=wds)
                        b.wait("tensor", wds, None)
                        if ht == 0 and ngrp >= 1:
                            b.wait("tensor", zev, zev_hist[-1])
                        for k in range(nd):
                            slast = (k == nd - 1)
                            b.do("tensor", lambda e, P=grp_banks[k], w=wd[pbuf], kk=k, h=ht, t=t5: e.matmul(
                                P[:], w[:, kk * 128:(kk + 1) * 128],
                                gbuf_r[:, h, t * 512:(t + 1) * 512],
                                start=(h == 0), stop=(h == NHT - 1)),
                                inc=(mslab if slast else None))
                        nslab += 1
                    ngrp += 1
                    b.wait("vector", mslab, nslab)
                    b.wait("vector", ptz, None)
                    for k in range(nd):
                        b.do("vector", lambda e, P=grp_banks[k], kk=k, dd=d0 + k: e.tensor_scalar(
                            zbuf_r[:, kk, 0:512], P[:], sds[:, dd:dd + 1], float(resid_scale),
                            ALU.mult, ALU.mult), inc=zev)
                    zev_hist.append(zev.n)
                    for q in range(4):
                        bank = ptr[nbank2 % 2]
                        if nbank2 >= 2:
                            b.wait("tensor", pzev, nbank2 - 1)
                        b.wait("tensor", zev, None)
                        for k in range(nd):
                            b.do("tensor", lambda e, bk=bank, kk=k, qq=q: e.transpose(
                                bk[:, kk * 128:(kk + 1) * 128], zbuf_r[:, kk, qq * 128:(qq + 1) * 128],
                                idn[:]), inc=(ptz if k == nd - 1 else None))
                        b.wait("scalar", ptz, None)
                        b.do("scalar", lambda e, bk=bank, n=nd, dd=d0, qq=q: e.copy(
                            ztm_r[:, qq, dd * 128:dd * 128 + n * 128], bk[:, 0:n * 128]), inc=pzev)
                        nbank2 += 1
                lr_snap = []
                for q in range(4):
                    ti_glob = c * TC + t5 * 512 + q * 128
                    b.wait("sync", adn, adn_base + (q - 1 if q >= 2 else 0))
                    b.dma("sync", pool[q % 2][:], resin[ti_glob:ti_glob + 128, :], lr[q % 2])
                    lr_snap.append(lr[q % 2].n)
                b.barrier([pzev], engines=("vector",))
                for q in range(4):
                    b.wait("vector", lr[q % 2], lr_snap[q])
                    b.wait("vector", st[q % 2], st_snap[q % 2])
                    b.do("vector", lambda e, qq=q: e.tensor_tensor(
                        pool[2 + qq % 2][:], ztm_r[:, qq, :], pool[qq % 2][:], ALU.add), inc=adn)
                    ti_glob = c * TC + t5 * 512 + q * 128
                    b.wait("sync", adn, None)
                    b.dma("sync", yout[ti_glob:ti_glob + 128, :], pool[2 + q % 2][:], st[q % 2])
                    st_snap[q % 2] = st[q % 2].n
        b.barrier([st[0], st[1]])


E = 8
INTER = 1408
NIT = INTER // 128


def build_l1ffn(T=2048):
    nc = bass.Bass("TRN2", target_bir_lowering=False)
    xin = nc.dram_tensor("x", [T, D], F32, kind="ExternalInput").ap()
    w1t = nc.dram_tensor("w1t", [9, NIT, 128, NDT, 128], BF16, kind="ExternalInput").ap()
    w3t = nc.dram_tensor("w3t", [9, NIT, 128, NDT, 128], BF16, kind="ExternalInput").ap()
    w2t = nc.dram_tensor("w2t", [9, NIT, 128, D], BF16, kind="ExternalInput").ap()
    gwT = nc.dram_tensor("gwT", [D, E], F32, kind="ExternalInput").ap()
    gb = nc.dram_tensor("gb", [1, E], F32, kind="ExternalInput").ap()
    idnp = nc.dram_tensor("idn", [128, 128], BF16, kind="ExternalInput").ap()
    idnfp = nc.dram_tensor("idnf", [128, 128], F32, kind="ExternalInput").ap()
    onesp = nc.dram_tensor("onesb", [1, 128], BF16, kind="ExternalInput").ap()
    yout = nc.dram_tensor("y", [T, D], F32, kind="ExternalOutput").ap()
    cwd = nc.dram_tensor("cwd", [E, 1024], F32, kind="Internal").ap()

    es = contextlib.ExitStack()
    with es:
        b = Build(nc, es)
        ctx = {}
        ctx["cwd"] = cwd
        ctx["idn"] = sbuf(nc, es, "idn_sb", (128, 128), BF16)
        ctx["idnf"] = sbuf(nc, es, "idnf_sb", (128, 128), F32)
        ctx["ones"] = sbuf(nc, es, "ones_sb", (1, 128), BF16)
        gws = sbuf(nc, es, "gws", (128, NDT * E), F32)
        gbb = sbuf(nc, es, "gbb", (128, E), F32)
        make_ln_ctx(b, ctx)
        lc = b.sem("lc")
        lcg = b.sem("lcg")
        b.dma("sync", ctx["idn"][:], idnp, lc)
        b.dma("sync", ctx["idnf"][:], idnfp, lc)
        b.dma("sync", ctx["ones"][:], onesp, lc)
        b.dma("sync", gws[:].rearrange("p (dt e) -> p dt e", dt=NDT),
              gwT.rearrange("(dt p) e -> p dt e", p=128), lc)
        b.dma("gpsimd", gbb[:], gb.partition_broadcast(128), lcg)
        b.barrier([lc, lcg])
        emit_l1ffn_body(b, ctx, xin, xin, yout, w1t, w3t, w2t, gws, gbb, T,
                        resid_scale=0.5, tag="m")
        b.finish()
    return nc


def emit_l1ffn_body(b, ctx, xin, resin, yout, w1t, w3t, w2t, gws, gbb, T,
                    resid_scale, tag):
    nc, es = b.nc, b.es
    TC = 1024
    nchunk = T // TC
    idn, idnf, ones = ctx["idn"], ctx["idnf"], ctx["ones"]

    with contextlib.ExitStack() as ph:
        xnf = sbuf(nc, ph, f"{tag}_xnf", (128, NDT * TC), BF16)
        xlo = sbuf(nc, ph, f"{tag}_xlo", (128, NDT * TC), BF16)
        hbuf = xlo
        ybuf = sbuf(nc, ph, f"{tag}_y", (128, NDT * TC), F32)
        w1b = [sbuf(nc, ph, f"{tag}_w1b{i}", (128, NDT * 128), BF16) for i in range(2)]
        w3b = [sbuf(nc, ph, f"{tag}_w3b{i}", (128, NDT * 128), BF16) for i in range(2)]
        w2b = [sbuf(nc, ph, f"{tag}_w2b{i}", (128, 768), BF16) for i in range(2)]
        pool = [sbuf(nc, ph, f"{tag}_pool{i}", (128, D), F32) for i in range(4)]
        t1 = sbuf(nc, ph, f"{tag}_t1", (128, 1024), BF16)
        t2 = sbuf(nc, ph, f"{tag}_t2", (128, 1024), BF16)
        t3 = sbuf(nc, ph, f"{tag}_t3", (128, 1024), BF16)
        ghi = sbuf(nc, ph, f"{tag}_ghi", (128, NDT * E), BF16)
        glo = sbuf(nc, ph, f"{tag}_glo", (128, NDT * E), BF16)
        gf32 = sbuf(nc, ph, f"{tag}_gf32", (128, NDT * E), F32)
        scT = sbuf(nc, ph, f"{tag}_scT", (8, TC), F32)
        sctk = sbuf(nc, ph, f"{tag}_sctk", (128, E), F32)
        bias_ = sbuf(nc, ph, f"{tag}_bias", (128, E), F32)
        cnt = [sbuf(nc, ph, f"{tag}_cnt{i}", (128, E), F32) for i in range(2)]
        cwt = sbuf(nc, ph, f"{tag}_cwt", (128, E), F32)
        cwr = sbuf(nc, ph, f"{tag}_cwr", (8, TC), F32)
        cwb = sbuf(nc, ph, f"{tag}_cwb", (128, TC), BF16)

        SL = ln_sems(b, tag)
        lw = [b.sem(f"{tag}_lw0"), b.sem(f"{tag}_lw1")]
        ld = [b.sem(f"{tag}_ld0"), b.sem(f"{tag}_ld1")]
        mme = b.sem(f"{tag}_mme")
        hev = b.sem(f"{tag}_hev")
        shev = b.sem(f"{tag}_shev")
        mslab = b.sem(f"{tag}_mslab")
        yev = b.sem(f"{tag}_yev")
        gsem = b.sem(f"{tag}_gsem")
        gsc = b.sem(f"{tag}_gsc")
        gmm = b.sem(f"{tag}_gmm")
        cwbs = b.sem(f"{tag}_cwbs")
        cwds = b.sem(f"{tag}_cwds")
        lr = [b.sem(f"{tag}_lr0"), b.sem(f"{tag}_lr1")]
        ptz = b.sem(f"{tag}_ptz")
        adn = b.sem(f"{tag}_adn")
        st = [b.sem(f"{tag}_st0"), b.sem(f"{tag}_st1")]

        st_snap = [0, 0]
        nuse = 0
        nslab = 0
        n_tp = 0
        lw_snap = {}
        ge1 = [0]

        xnf_r = xnf[:].rearrange("p (dt tok) -> p dt tok", dt=NDT)
        xlo_r = xlo[:].rearrange("p (dt tok) -> p dt tok", dt=NDT)
        hbuf_r = hbuf[:, 0:NIT * TC].rearrange("p (it tok) -> p it tok", it=NIT)
        ybuf_r = ybuf[:].rearrange("p (dt tok) -> p dt tok", dt=NDT)
        gws_r = gws[:].rearrange("p (dt e) -> p dt e", dt=NDT)
        ghi_r = ghi[:].rearrange("p (dt e) -> p dt e", dt=NDT)
        glo_r = glo[:].rearrange("p (dt e) -> p dt e", dt=NDT)
        gf32_r = gf32[:].rearrange("p (dt e) -> p dt e", dt=NDT)
        w1b_r = [w[:].rearrange("p (dt c) -> p dt c", dt=NDT) for w in w1b]
        w3b_r = [w[:].rearrange("p (dt c) -> p dt c", dt=NDT) for w in w3b]

        b.do("vector", lambda e: e.tensor_copy(ghi[:], gws[:]))
        b.do("vector", lambda e: e.tensor_copy(gf32[:], ghi[:]))
        b.do("vector", lambda e: e.tensor_tensor(gf32[:], gws[:], gf32[:], ALU.subtract))
        b.do("vector", lambda e: e.tensor_copy(glo[:], gf32[:]), inc=gsem)

        for c in range(nchunk):
            with contextlib.ExitStack() as ps:
                ptr = [psum(nc, ps, f"{tag}_ptr{c}{i}", (128, 1024), BF16) for i in range(2)]
                b.barrier([mslab, adn, st[0], st[1], yev, ptz, hev, shev, gsem, gsc, cwbs, mme])
                emit_ln(b, ctx, xin[c * TC:(c + 1) * TC, :], xnf, TC, SL, pool, ptr, xlo=xlo)
                b.barrier([SL["pev"], SL["xnd"], SL["sqd"]])

            with contextlib.ExitStack() as ps:
                pg8 = psum(nc, ps, f"{tag}_pg8{c}", (8, 512), F32)
                ptk = psum(nc, ps, f"{tag}_ptk{c}", (128, 8), F32)
                pte = psum(nc, ps, f"{tag}_pte{c}", (8, 128), F32)
                for t5 in range(2):
                    b.wait("tensor", gsem, None)
                    for wi, (wmat, xmat) in enumerate(((ghi_r, xnf_r), (ghi_r, xlo_r), (glo_r, xnf_r))):
                        for dt in range(NDT):
                            b.do("tensor", lambda e, w=wmat, x=xmat, d=dt, t=t5,
                                 f=(wi == 0 and dt == 0), s2=(wi == 2 and dt == NDT - 1): e.matmul(
                                pg8[:], w[:, d, :], x[:, d, t * 512:(t + 1) * 512],
                                start=f, stop=s2),
                                inc=(gmm if (wi == 2 and dt == NDT - 1) else None))
                    b.wait("vector", gmm, None)
                    sl = scT[0:8, t5 * 512:(t5 + 1) * 512]
                    b.do("vector", lambda e, o=sl: e.tensor_scalar(o, pg8[:], -1.0, None, ALU.mult))
                    b.do("vector", lambda e, o=sl: e.tensor_tensor(o, pg8[:], o, ALU.min), inc=gsem)
                    b.wait("scalar", gsem, None)
                    b.do("scalar", lambda e, o=sl: e.activation(o, o, AF.Exp), inc=gsc)
                    b.wait("vector", gsc, None)
                    b.do("vector", lambda e, o=sl: e.tensor_scalar(o, o, 1.0, None, ALU.add))
                    b.do("vector", lambda e, o=sl: e.tensor_scalar(o, o, 0.0, None, ALU.max), inc=gsem)
                    b.wait("scalar", gsem, None)
                    b.do("scalar", lambda e, o=sl: e.activation(o, o, AF.Ln), inc=gsc)
                    b.wait("vector", gsc, None)
                    b.do("vector", lambda e, o=sl: e.tensor_scalar(t3[0:8, 0:512], pg8[:], 0.0, None, ALU.max))
                    b.do("vector", lambda e, o=sl: e.tensor_tensor(o, o, t3[0:8, 0:512], ALU.add), inc=gsem)
                    b.wait("scalar", gsem, None)
                    b.do("scalar", lambda e, o=sl: e.activation(o, o, AF.Sqrt), inc=gsc)
                b.wait("tensor", gsc, None)
                for blk in range(TC // 128):
                    b.wait("tensor", gsem, None)
                    b.wait("tensor", gsc, None)
                    b.do("tensor", lambda e, k=blk: e.transpose(
                        ptk[:], scT[0:8, k * 128:(k + 1) * 128], idnf[0:8, 0:8]), inc=gmm)
                    b.wait("vector", gmm, None)
                    b.do("vector", lambda e: e.tensor_copy(sctk[:], ptk[:]))
                    b.do("vector", lambda e: e.tensor_tensor(bias_[:], sctk[:], gbb[:], ALU.add))
                    b.do("vector", lambda e: e.memset(cnt[0][:], 0.0))
                    for ee in range(E):
                        sb_, db_ = cnt[ee % 2], cnt[(ee + 1) % 2]
                        b.do("vector", lambda e, s=sb_, d=db_, k=ee: e.scalar_tensor_tensor(
                            d[:], bias_[:], bias_[:, k:k + 1], s[:], ALU.is_lt, ALU.add))
                    b.do("vector", lambda e: e.tensor_scalar(cnt[1][:], cnt[0][:], 2.0, None, ALU.is_lt))
                    b.do("vector", lambda e: e.tensor_tensor(cwt[:], cnt[1][:], sctk[:], ALU.mult), inc=gsem)
                    b.wait("tensor", gsem, None)
                    b.do("tensor", lambda e, k=blk: e.transpose(pte[:], cwt[0:128, 0:8], idnf[:]), inc=gmm)
                    b.wait("scalar", gmm, None)
                    b.do("scalar", lambda e, k=blk: e.copy(cwr[0:8, k * 128:(k + 1) * 128], pte[0:8, :]), inc=gsc)
                b.wait("sync", gsc, None)
                b.dma("sync", ctx["cwd"], cwr[:], cwds)
                b.barrier([gsc, gsem, gmm, cwds])

            with contextlib.ExitStack() as ps:
                p1 = [psum(nc, ps, f"{tag}_p1{c}{i}", (128, 512), F32) for i in range(2)]
                p3 = [psum(nc, ps, f"{tag}_p3{c}{i}", (128, 512), F32) for i in range(2)]
                pz2 = [psum(nc, ps, f"{tag}_pz2{c}{i}", (128, 512), F32) for i in range(2)]
                grp_banks = [p1[0], p1[1], p3[0], p3[1], pz2[0], pz2[1]]
                GRPS = [(0, 6), (6, 12), (12, 16)]
                zev_like = []
                for eidx in range(9):
                    shared = eidx == 8
                    if not shared:
                        b.wait("gpsimd", hev, None)
                        b.dma("gpsimd", cwb[:], ctx["cwd"][eidx:eidx + 1, :].partition_broadcast(128), cwbs)
                    if eidx >= 1:
                        b.wait("vector", mslab, nslab)
                        b.wait("scalar", mslab, nslab)
                        b.wait("tensor", yev, None)
                    for it in range(NIT):
                        pb = (ge1[0] + eidx * NIT + it) % 2
                        gs = ge1[0] + eidx * NIT + it
                        if eidx == 0 and it == 0:
                            b.dma("gpsimd", w1b[gs % 2][:], w1t[eidx, 0], lw[gs % 2])
                            b.dma("gpsimd", w3b[gs % 2][:], w3t[eidx, 0], lw[gs % 2])
                            lw_snap[gs] = lw[gs % 2].n
                        ne, ni = (eidx, it + 1) if it + 1 < NIT else (eidx + 1, 0)
                        if ne < 9:
                            nb = (gs + 1) % 2
                            b.wait("gpsimd", mme, 2 * gs)
                            b.dma("gpsimd", w1b[nb][:], w1t[ne, ni], lw[nb])
                            b.dma("gpsimd", w3b[nb][:], w3t[ne, ni], lw[nb])
                            lw_snap[gs + 1] = lw[nb].n
                        b.wait("tensor", lw[gs % 2], lw_snap[gs])
                        for t5 in range(2):
                            pu_ = (eidx * NIT + it) * 2 + t5
                            if nuse >= 2:
                                b.wait("tensor", hev, nuse - 1)
                                b.wait("tensor", shev, 2 * (nuse - 1))
                            for dt in range(NDT):
                                b.do("tensor", lambda e, P=p1[t5], w=w1b_r[pb], d=dt, t=t5: e.matmul(
                                    P[:], w[:, d, :], xnf_r[:, d, t * 512:(t + 1) * 512],
                                    start=(d == 0), stop=(d == NDT - 1)))
                            for dt in range(NDT):
                                b.do("tensor", lambda e, P=p3[t5], w=w3b_r[pb], d=dt, t=t5: e.matmul(
                                    P[:], w[:, d, :], xnf_r[:, d, t * 512:(t + 1) * 512],
                                    start=(d == 0), stop=(d == NDT - 1)),
                                    inc=(mme if dt == NDT - 1 else None))
                            b.wait("scalar", mme, None)
                            b.wait("scalar", hev, max(0, nuse - 1))
                            b.do("scalar", lambda e, P=p1[t5], t=t5: e.activation(
                                t1[:, t * 512:(t + 1) * 512], P[:], AF.Silu), inc=shev)
                            b.do("scalar", lambda e, P=p3[t5], t=t5: e.copy(
                                t3[:, t * 512:(t + 1) * 512], P[:]), inc=shev)
                            b.wait("vector", shev, None)
                            if not shared:
                                b.wait("vector", cwbs, None)
                            if shared:
                                b.do("vector", lambda e, i2=it, t=t5: e.tensor_tensor(
                                    hbuf_r[:, i2, t * 512:(t + 1) * 512],
                                    t1[:, t * 512:(t + 1) * 512], t3[:, t * 512:(t + 1) * 512],
                                    ALU.mult), inc=hev)
                            else:
                                b.do("vector", lambda e, t=t5: e.tensor_tensor(
                                    t2[:, t * 512:(t + 1) * 512], t1[:, t * 512:(t + 1) * 512],
                                    cwb[:, t * 512:(t + 1) * 512], ALU.mult))
                                b.do("vector", lambda e, i2=it, t=t5: e.tensor_tensor(
                                    hbuf_r[:, i2, t * 512:(t + 1) * 512],
                                    t2[:, t * 512:(t + 1) * 512], t3[:, t * 512:(t + 1) * 512],
                                    ALU.mult), inc=hev)
                            nuse += 1
                    for t5 in range(2):
                        for gi, (d0, d1) in enumerate(GRPS):
                            nd = d1 - d0
                            for it in range(NIT):
                                pb2 = nslab % 2
                                if nslab >= 2:
                                    b.wait("gpsimd", mslab, nslab - 1)
                                b.dma("gpsimd", w2b[pb2][:, 0:nd * 128],
                                      w2t[eidx, it, :, d0 * 128:d1 * 128], ld[pb2])
                                b.wait("tensor", ld[pb2], None)
                                if it == 0:
                                    if zev_like:
                                        b.wait("tensor", yev, zev_like[-1])
                                    if gi == 0 and t5 == 0:
                                        b.wait("tensor", hev, None)
                                for k in range(nd):
                                    b.do("tensor", lambda e, P=grp_banks[k], w=w2b[pb2], kk=k, i2=it, t=t5: e.matmul(
                                        P[:], w[:, kk * 128:(kk + 1) * 128],
                                        hbuf_r[:, i2, t * 512:(t + 1) * 512],
                                        start=(i2 == 0), stop=(i2 == NIT - 1)),
                                        inc=(mslab if k == nd - 1 else None))
                                nslab += 1
                            b.wait("vector", mslab, nslab)
                            for k in range(nd):
                                if eidx == 0:
                                    b.do("vector", lambda e, P=grp_banks[k], dd=d0 + k, t=t5: e.tensor_scalar(
                                        ybuf_r[:, dd, t * 512:(t + 1) * 512], P[:], 1.0, None, ALU.mult),
                                        inc=yev)
                                else:
                                    b.do("vector", lambda e, P=grp_banks[k], dd=d0 + k, t=t5: e.tensor_tensor(
                                        ybuf_r[:, dd, t * 512:(t + 1) * 512], P[:],
                                        ybuf_r[:, dd, t * 512:(t + 1) * 512], ALU.add), inc=yev)
                            zev_like.append(yev.n)
                ge1[0] += 9 * NIT

            with contextlib.ExitStack() as ps:
                ptf = [psum(nc, ps, f"{tag}_ptf{c}{i}", (128, 512), F32) for i in range(2)]
                b.barrier([yev], engines=("tensor", "sync", "vector"))
                adn_base = adn.n
                lr_snap = {}
                for q in range(2):
                    ti_glob = c * TC + q * 128
                    b.dma("sync", pool[q % 2][:], resin[ti_glob:ti_glob + 128, :], lr[q % 2])
                    lr_snap[q] = lr[q % 2].n
                for q in range(8):
                    for dgrp in range(4):
                        bank = ptf[n_tp % 2]
                        if n_tp >= 2:
                            b.wait("tensor", adn, n_tp - 1)
                        for k in range(4):
                            dt = dgrp * 4 + k
                            b.do("tensor", lambda e, bk=bank, kk=k, d=dt, qq=q: e.transpose(
                                bk[:, kk * 128:(kk + 1) * 128].bitcast(F32),
                                ybuf_r[:, d, qq * 128:(qq + 1) * 128], idnf[:]),
                                inc=(ptz if k == 3 else None))
                        b.wait("vector", ptz, None)
                        b.wait("vector", lr[q % 2], lr_snap[q])
                        b.wait("vector", st[q % 2], st_snap[q % 2])
                        b.do("vector", lambda e, bk=bank, g2=dgrp, qq=q: e.scalar_tensor_tensor(
                            pool[2 + qq % 2][:, g2 * 512:(g2 + 1) * 512], bk[:],
                            float(resid_scale), pool[qq % 2][:, g2 * 512:(g2 + 1) * 512],
                            ALU.mult, ALU.add), inc=adn)
                        n_tp += 1
                    ti_glob = c * TC + q * 128
                    b.wait("sync", adn, None)
                    b.dma("sync", yout[ti_glob:ti_glob + 128, :], pool[2 + q % 2][:], st[q % 2])
                    st_snap[q % 2] = st[q % 2].n
                    if q + 2 < 8:
                        ti2 = c * TC + (q + 2) * 128
                        b.dma("sync", pool[q % 2][:], resin[ti2:ti2 + 128, :], lr[q % 2])
                        lr_snap[q + 2] = lr[q % 2].n
        b.barrier([st[0], st[1]])


HD = 128
ISQ = 0.08838834764831845


def build_l1attn(L=4096, NHC=8):
    nc = bass.Bass("TRN2", target_bir_lowering=False)
    xin = nc.dram_tensor("x", [L, D], F32, kind="ExternalInput").ap()
    xres = nc.dram_tensor("xres", [L, NHC * HD], F32, kind="ExternalInput").ap()
    wq = nc.dram_tensor("wq", [D, NHC * HD], F32, kind="ExternalInput").ap()
    wqs = nc.dram_tensor("wqs", [D, NHC * HD], F32, kind="ExternalInput").ap()
    wk = nc.dram_tensor("wk", [D, HD], F32, kind="ExternalInput").ap()
    wks = nc.dram_tensor("wks", [D, HD], F32, kind="ExternalInput").ap()
    wv = nc.dram_tensor("wv", [D, HD], F32, kind="ExternalInput").ap()
    rc = nc.dram_tensor("rc", [HD, L], BF16, kind="ExternalInput").ap()
    rs = nc.dram_tensor("rs", [HD, L], BF16, kind="ExternalInput").ap()
    cmask = nc.dram_tensor("cmask", [4, 128, 512], BF16, kind="ExternalInput").ap()
    onescol = nc.dram_tensor("onescol", [128, 1], BF16, kind="ExternalInput").ap()
    idnp = nc.dram_tensor("idn", [128, 128], BF16, kind="ExternalInput").ap()
    yout = nc.dram_tensor("y", [L, NHC * HD], F32, kind="ExternalOutput").ap()
    csc = nc.dram_tensor("csc", [1, 8], F32, kind="Internal").ap()

    es = contextlib.ExitStack()
    with es:
        b = Build(nc, es)
        ctx = {}
        ctx["idn"] = sbuf(nc, es, "idn_sb", (128, 128), BF16)
        make_ln_ctx(b, ctx)
        lc = b.sem("lc")
        b.dma("sync", ctx["idn"][:], idnp, lc)
        emit_l1attn_body(b, ctx, xin, xres, yout, wq, wqs, wk, wks, wv, rc, rs,
                         cmask, onescol, csc, L, NHC, lc, tag="a")
        b.finish()
    return nc


def emit_l1attn_body(b, ctx, xin, xres, yout, wq, wqs, wk, wks, wv, rc, rs,
                     cmask, onescol, csc, L, NHC, lc, tag):
    nc, es = b.nc, b.es
    TC = 1024
    nchunk = L // TC
    NLT = L // 128
    idn = ctx["idn"]

    with contextlib.ExitStack() as ph:
        xnf = sbuf(nc, ph, f"{tag}_xnf", (128, NDT * 512), BF16)
        qr = sbuf(nc, ph, f"{tag}_qr", (128, NHC * L), BF16)
        kr = sbuf(nc, ph, f"{tag}_kr", (128, L), BF16)
        vtm = sbuf(nc, ph, f"{tag}_vtm", (128, NLT * 136), BF16)
        rcb = sbuf(nc, ph, f"{tag}_rcb", (128, L), BF16)
        rsb = sbuf(nc, ph, f"{tag}_rsb", (128, L), BF16)
        wqb = [sbuf(nc, ph, f"{tag}_wqb{i}", (128, 4 * NHC * HD), BF16) for i in range(2)]
        kwb = sbuf(nc, ph, f"{tag}_kwb", (128, NDT * 3 * HD), BF16)
        qtmp = sbuf(nc, ph, f"{tag}_qtmp", (128, NHC * 512), BF16)
        ta = sbuf(nc, ph, f"{tag}_ta", (128, 512), F32)
        tb = sbuf(nc, ph, f"{tag}_tb", (128, 512), F32)
        pbuf = sbuf(nc, ph, f"{tag}_pbuf", (128, 2 * 512), BF16)
        cmb = sbuf(nc, ph, f"{tag}_cmb", (128, 4 * 512), BF16)
        onesc = sbuf(nc, ph, f"{tag}_ones", (128, 1), BF16)
        qsq = sbuf(nc, ph, f"{tag}_qsq", (128, 512), BF16)
        nrm = sbuf(nc, ph, f"{tag}_nrm", (1, 512), F32)
        mx = sbuf(nc, ph, f"{tag}_mx", (1, 4), F32)
        negc = sbuf(nc, ph, f"{tag}_negc", (128, 1), F32)
        obuf = sbuf(nc, ph, f"{tag}_obuf", (128, 4 * NHC * HD), BF16)
        dnr = sbuf(nc, ph, f"{tag}_dnr", (128, 4), F32)
        pool = [sbuf(nc, ph, f"{tag}_pool{i}", (128, D), F32) for i in range(3)]

        SL = ln_sems(b, tag)
        lwq = [b.sem(f"{tag}_lwq0"), b.sem(f"{tag}_lwq1")]
        lk = b.sem(f"{tag}_lk")
        lro = b.sem(f"{tag}_lro")
        mmq = b.sem(f"{tag}_mmq")
        qev = b.sem(f"{tag}_qev")
        rpd = b.sem(f"{tag}_rpd")
        nsc = b.sem(f"{tag}_nsc")
        nvc = b.sem(f"{tag}_nvc")
        nmm = b.sem(f"{tag}_nmm")
        ncd = b.sem(f"{tag}_ncd")
        ncg = b.sem(f"{tag}_ncg")
        pex = b.sem(f"{tag}_pex")
        pmk = b.sem(f"{tag}_pmk")
        mms = b.sem(f"{tag}_mms")
        mma = b.sem(f"{tag}_mma")
        oev = b.sem(f"{tag}_oev")
        lr = [b.sem(f"{tag}_lr0"), b.sem(f"{tag}_lr1")]
        adn = b.sem(f"{tag}_adn")
        st = [b.sem(f"{tag}_st0"), b.sem(f"{tag}_st1")]

        qr_r = qr[:].rearrange("p (h l) -> p h l", h=NHC)
        vtm_r = vtm[:].rearrange("p (t d) -> p t d", t=NLT)
        b.do("vector", lambda e: e.memset(vtm_r[:, :, 128:136], 0.0))
        b.do("vector", lambda e: e.memset(vtm_r[:, :, 128:129], 1.0))
        obuf_r = obuf[:].rearrange("p (t f) -> p t f", t=4)
        wq_r = wq.rearrange("(dt p) f -> p dt f", p=128)
        wqs_r = wqs.rearrange("(dt p) f -> p dt f", p=128)
        kwb_r = kwb[:].rearrange("p (dt c) -> p dt c", dt=NDT)

        b.dma("sync", rcb[:], rc, lro)
        b.dma("sync", rsb[:], rs, lro)
        b.dma("sync", cmb[:].rearrange("p (a f) -> p a f", a=4),
              cmask.rearrange("a p f -> p a f"), lro)
        b.dma("sync", onesc[:], onescol, lro)
        b.dma("gpsimd", kwb_r[:, :, 0:HD], wk.rearrange("(dt p) f -> p dt f", p=128), lk)
        b.dma("gpsimd", kwb_r[:, :, HD:2 * HD], wks.rearrange("(dt p) f -> p dt f", p=128), lk)
        b.dma("gpsimd", kwb_r[:, :, 2 * HD:3 * HD], wv.rearrange("(dt p) f -> p dt f", p=128), lk)
        b.barrier([lro, lk, lc])

        TCA = 512
        qmm_snap = {}
        lwq_snap = [0, 0]
        nqt = 0
        with contextlib.ExitStack() as ps:
            ptr = [psum(nc, ps, f"{tag}_ptr{i}", (128, 1024), BF16) for i in range(2)]
            pq = [psum(nc, ps, f"{tag}_pq{i}", (128, 512), F32) for i in range(6)]
            for c in range(L // TCA):
                l0 = c * TCA
                b.barrier([mmq, qev, rpd])
                emit_ln(b, ctx, xin[l0:l0 + TCA, :], xnf, TCA, SL, pool, ptr)
                b.barrier([SL["pev"], SL["xnd"], SL["sqd"]])
                for pss, wsrc in ((0, wq_r), (1, wqs_r)):
                    for ots in (range(0, 6), range(6, 8)):
                        for qt in range(4):
                            wb = wqb[nqt % 2]
                            if nqt >= 2:
                                b.wait("gpsimd", mmq, qmm_snap[nqt - 2])
                            b.dma("gpsimd", wb[:].rearrange("p (dt f) -> p dt f", dt=4),
                                  wsrc[:, qt * 4:(qt + 1) * 4, :], lwq[nqt % 2])
                            lwq_snap[nqt % 2] = lwq[nqt % 2].n
                            b.wait("tensor", lwq[nqt % 2], lwq_snap[nqt % 2])
                            if qt == 0:
                                b.wait("tensor", qev, None)
                                b.wait("tensor", rpd, None)
                            wbr = wb[:].rearrange("p (dt f) -> p dt f", dt=4)
                            for ot in ots:
                                for d4 in range(4):
                                    dt = qt * 4 + d4
                                    xsl = xnf[:, dt * TCA: dt * TCA + 512]
                                    b.do("tensor", lambda e, P=pq[ot % 6], w=wbr, dd=d4, o=ot, xs=xsl,
                                         f=(qt == 0 and d4 == 0), s2=(qt == 3 and d4 == 3): e.matmul(
                                        P[:], w[:, dd, o * 128:(o + 1) * 128], xs,
                                        start=f, stop=s2),
                                        inc=(mmq if (ot == ots[-1] and d4 == 3) else None))
                            qmm_snap[nqt] = mmq.n
                            nqt += 1
                        b.wait("scalar", mmq, None)
                        b.wait("vector", mmq, None)
                        for ot in ots:
                            if pss == 0:
                                b.wait("scalar", rpd, rpd.n)
                                b.do("scalar", lambda e, P=pq[ot % 6], o=ot: e.copy(
                                    qtmp[:, o * 512:(o + 1) * 512], P[:]), inc=qev)
                            else:
                                b.wait("vector", qev, None)
                                b.do("vector", lambda e, o=ot, z=l0: e.tensor_tensor(
                                    ta[:], qtmp[:, o * 512:(o + 1) * 512], rcb[:, z:z + 512], ALU.mult))
                                b.do("vector", lambda e, P=pq[ot % 6], z=l0: e.tensor_tensor(
                                    tb[:], P[:], rsb[:, z:z + 512], ALU.mult))
                                b.do("vector", lambda e, o=ot, z=l0: e.tensor_tensor(
                                    qr_r[:, o, z:z + 512], ta[:], tb[:], ALU.add), inc=rpd)
                b.wait("tensor", rpd, None)
                b.wait("tensor", qev, None)
                for kk, coff in ((0, 0), (1, HD)):
                    for dt in range(NDT):
                        b.do("tensor", lambda e, P=pq[kk], d=dt, co=coff: e.matmul(
                            P[:], kwb_r[:, d, co:co + HD], xnf[:, d * TCA: d * TCA + 512],
                            start=(d == 0), stop=(d == NDT - 1)),
                            inc=(mmq if dt == NDT - 1 else None))
                b.wait("scalar", mmq, None)
                b.wait("scalar", rpd, rpd.n)
                b.do("scalar", lambda e: e.copy(qtmp[:, 0:512], pq[0][:]), inc=qev)
                b.wait("vector", qev, None)
                b.do("vector", lambda e, z=l0: e.tensor_tensor(ta[:], qtmp[:, 0:512], rcb[:, z:z + 512], ALU.mult))
                b.do("vector", lambda e, z=l0: e.tensor_tensor(tb[:], pq[1][:], rsb[:, z:z + 512], ALU.mult))
                b.do("vector", lambda e, z=l0: e.tensor_tensor(kr[:, z:z + 512], ta[:], tb[:], ALU.add), inc=rpd)
                b.wait("tensor", rpd, None)
                for q4 in range(4):
                    tt = l0 // 128 + q4
                    P = pq[2 + q4 % 2]
                    if q4 >= 2:
                        b.wait("tensor", qev, None)
                    for dt in range(NDT):
                        b.do("tensor", lambda e, Pp=P, d=dt, z=q4 * 128: e.matmul(
                            Pp[:, 0:HD], xnf[:, d * TCA + z: d * TCA + z + 128],
                            kwb_r[:, d, 2 * HD:3 * HD],
                            start=(d == 0), stop=(d == NDT - 1)),
                            inc=(mmq if dt == NDT - 1 else None))
                    b.wait("scalar", mmq, None)
                    b.do("scalar", lambda e, Pp=P, t=tt: e.copy(vtm_r[:, t, 0:HD], Pp[:, 0:HD]), inc=qev)

        b.barrier([mmq, qev, rpd])
        with contextlib.ExitStack() as ps:
            pnr = psum(nc, ps, f"{tag}_pnr", (1, 512), F32)
            pnr2 = psum(nc, ps, f"{tag}_pnr2", (1, 512), F32)
            b.do("vector", lambda e: e.memset(mx[:], 0.0), inc=nvc)
            nslice = 0
            for src_kind in range(2):
                nsl = NHC * (L // 512) if src_kind == 0 else (L // 512)
                for s in range(nsl):
                    if src_kind == 0:
                        h, z = divmod(s, L // 512)
                        view = qr_r[:, h, z * 512:(z + 1) * 512]
                    else:
                        view = kr[:, s * 512:(s + 1) * 512]
                    P = pnr if nslice % 2 == 0 else pnr2
                    b.wait("scalar", nmm, None)
                    b.wait("scalar", nvc, None)
                    b.do("scalar", lambda e, v=view: e.activation(qsq[:], v, AF.Square), inc=nsc)
                    b.wait("tensor", nsc, None)
                    b.wait("tensor", nvc, None)
                    b.do("tensor", lambda e, Pp=P: e.matmul(Pp[:], onesc[:], qsq[:], start=True, stop=True), inc=nmm)
                    b.wait("vector", nmm, None)
                    b.do("vector", lambda e, Pp=P: e.reduce_max(nrm[0:1, 0:1], Pp[:], axis=AXX), inc=nvc)
                    b.wait("vector", nvc, None)
                    dst = mx[0:1, src_kind:src_kind + 1]
                    b.do("vector", lambda e, dd=dst: e.tensor_tensor(dd, dd, nrm[0:1, 0:1], ALU.max), inc=nvc)
                    nslice += 1
            b.do("vector", lambda e: e.tensor_tensor(mx[0:1, 2:3], mx[0:1, 0:1], mx[0:1, 1:2], ALU.mult), inc=nvc)
            b.wait("scalar", nvc, None)
            b.do("scalar", lambda e: e.activation(mx[0:1, 3:4], mx[0:1, 2:3], AF.Sqrt), inc=nsc)
            b.wait("vector", nsc, None)
            b.do("vector", lambda e: e.tensor_scalar(mx[0:1, 3:4], mx[0:1, 3:4], -ISQ, None, ALU.mult), inc=nvc)
            b.wait("sync", nvc, None)
            b.dma("sync", csc[0:1, 0:1], mx[0:1, 3:4], ncd)
            b.wait("gpsimd", ncd, None)
            b.dma("gpsimd", negc[:], csc[0:1, 0:1].partition_broadcast(128), ncg)
            b.barrier([ncg, nvc, nsc, nmm])

        with contextlib.ExitStack() as ps:
            psc = [psum(nc, ps, f"{tag}_psc{i}", (128, 512), F32) for i in range(2)]
            pav = [psum(nc, ps, f"{tag}_pav{i}", (128, 512), F32) for i in range(4)]
            adn_base = 0
            lr_snap = {}
            st_snap = [0, 0]
            njt = 0
            for lc in range(L // 512):
                l0 = lc * 512
                jmax = (lc + 1) * 4
                for h in range(NHC):
                    pairidx = lc * NHC + h
                    if pairidx >= 1:
                        b.wait("tensor", oev, None)
                    for jt in range(jmax):
                        PS = psc[njt % 2]
                        if njt >= 2:
                            b.wait("tensor", pex, njt - 1)
                        b.do("tensor", lambda e, Pp=PS, j=jt, z=l0, hh=h: e.matmul(
                            Pp[:], kr[:, j * 128:(j + 1) * 128], qr_r[:, hh, z:z + 512],
                            start=True, stop=True), inc=mms)
                        b.wait("scalar", mms, None)
                        if njt >= 2:
                            b.wait("scalar", mma, njt - 1)
                        b.do("scalar", lambda e, Pp=PS, sl=pbuf[:, (njt % 2) * 512:(njt % 2 + 1) * 512]: e.activation(
                            sl, Pp[:], AF.Exp, bias=negc[:, 0:1], scale=ISQ), inc=pex)
                        diag = jt >= lc * 4
                        if diag:
                            k = jt - lc * 4
                            b.wait("vector", pex, None)
                            b.do("vector", lambda e, sl=pbuf[:, (njt % 2) * 512:(njt % 2 + 1) * 512], kk=k: e.tensor_tensor(
                                sl, sl, cmb[:, kk * 512:(kk + 1) * 512], ALU.mult), inc=pmk)
                            b.wait("tensor", pmk, None)
                        else:
                            b.wait("tensor", pex, None)
                        first = jt == 0
                        last = jt == jmax - 1
                        for lt in range(4):
                            b.do("tensor", lambda e, Pp=pav[lt], sl=pbuf[:, (njt % 2) * 512:], l2=lt, j=jt,
                                 f=first, s2=last: e.matmul(
                                Pp[:, 0:129], sl[:, l2 * 128:(l2 + 1) * 128],
                                vtm_r[:, j, 0:129], start=f, stop=s2),
                                inc=(mma if lt == 3 else None))
                        njt += 1
                    b.wait("vector", mma, None)
                    for lt in range(4):
                        b.do("vector", lambda e, l2=lt: e.reciprocal(dnr[:, l2:l2 + 1], pav[l2][:, 128:129]))
                    b.do("vector", lambda e: e.nop())
                    b.wait("vector", mma, None)
                    for lt in range(4):
                        b.do("vector", lambda e, l2=lt, hh=h: e.tensor_scalar(
                            obuf_r[:, l2, hh * 128:(hh + 1) * 128], pav[l2][:, 0:128],
                            dnr[:, l2:l2 + 1], None, ALU.mult), inc=oev)
                for lt in range(4):
                    gl = lc * 4 + lt
                    p = gl % 2
                    rxv = pool[0][:, p * 1024:(p + 1) * 1024]
                    xov = pool[1][:, p * 1024:(p + 1) * 1024]
                    b.wait("sync", adn, max(0, adn.n - 1))
                    b.dma("sync", rxv, xres[gl * 128:(gl + 1) * 128, :], lr[p])
                    lr_snap[gl] = lr[p].n
                    b.wait("vector", lr[p], lr_snap[gl])
                    b.wait("vector", st[p], st_snap[p])
                    b.do("vector", lambda e, l2=lt, rv=rxv, xv=xov: e.tensor_tensor(
                        xv, obuf_r[:, l2, :], rv, ALU.add), inc=adn)
                    b.wait("sync", adn, None)
                    b.dma("sync", yout[gl * 128:(gl + 1) * 128, :], xov, st[p])
                    st_snap[p] = st[p].n
        b.barrier([st[0], st[1]])


DSTATE = 16


def build_l0attn(L=4096, NDH=8):
    nc = bass.Bass("TRN2", target_bir_lowering=False)
    xin = nc.dram_tensor("x", [L, D], F32, kind="ExternalInput").ap()
    xres = nc.dram_tensor("xres", [L, NDH * 128], F32, kind="ExternalInput").ap()
    wdloT = nc.dram_tensor("wdloT", [D, 128], F32, kind="ExternalInput").ap()
    wbT = nc.dram_tensor("wbT", [D, DSTATE], F32, kind="ExternalInput").ap()
    wcT = nc.dram_tensor("wcT", [D, DSTATE], F32, kind="ExternalInput").ap()
    bdlo = nc.dram_tensor("bdlo", [128, 1], F32, kind="ExternalInput").ap()
    bbc = nc.dram_tensor("bbc", [DSTATE, 2], F32, kind="ExternalInput").ap()
    wdtT = nc.dram_tensor("wdtT", [128, NDH * 128], F32, kind="ExternalInput").ap()
    bdt = nc.dram_tensor("bdt", [128, NDH], F32, kind="ExternalInput").ap()
    alog = nc.dram_tensor("alog", [128, NDH * DSTATE], F32, kind="ExternalInput").ap()
    dvec = nc.dram_tensor("dvec", [128, NDH], F32, kind="ExternalInput").ap()
    idnp = nc.dram_tensor("idn", [128, 128], BF16, kind="ExternalInput").ap()
    idnfp = nc.dram_tensor("idnf", [128, 128], F32, kind="ExternalInput").ap()
    yout = nc.dram_tensor("y", [L, NDH * 128], F32, kind="ExternalOutput").ap()
    delta_d = nc.dram_tensor("delta_d", [NDH * 128, L], BF16, kind="Internal").ap()
    xnh_d = nc.dram_tensor("xnh_d", [NDH * 128, L], BF16, kind="Internal").ap()
    bc_d = nc.dram_tensor("bc_d", [2 * DSTATE, L], F32, kind="Internal").ap()

    es = contextlib.ExitStack()
    with es:
        b = Build(nc, es)
        ctx = {}
        ctx["idn"] = sbuf(nc, es, "idn_sb", (128, 128), BF16)
        ctx["idnf"] = sbuf(nc, es, "idnf_sb", (128, 128), F32)
        make_ln_ctx(b, ctx)
        lc = b.sem("lc")
        b.dma("sync", ctx["idn"][:], idnp, lc)
        b.dma("sync", ctx["idnf"][:], idnfp, lc)
        emit_l0attn_body(b, ctx, xin, xres, yout, wdloT, wbT, wcT, bdlo, bbc,
                         wdtT, bdt, alog, dvec, delta_d, xnh_d, bc_d, L, NDH,
                         lc, dh_off=0, tag="s")
        b.finish()
    return nc


def emit_l0attn_body(b, ctx, xin, xres, yout, wdloT, wbT, wcT, bdlo, bbc,
                     wdtT, bdt, alog, dvec, delta_d, xnh_d, bc_d, L, NDH, lc,
                     dh_off, tag):
    nc, es = b.nc, b.es
    TCA = 512
    idn, idnf = ctx["idn"], ctx["idnf"]

    with contextlib.ExitStack() as ph:
        xnf = sbuf(nc, ph, f"{tag}_xnf", (128, NDT * TCA), BF16)
        pool = [sbuf(nc, ph, f"{tag}_pool{i}", (128, D), F32) for i in range(3)]
        wdl = sbuf(nc, ph, f"{tag}_wdl", (128, NDT * 128), BF16)
        wbc = sbuf(nc, ph, f"{tag}_wbc", (128, NDT * 2 * DSTATE), BF16)
        wdt = sbuf(nc, ph, f"{tag}_wdt", (128, NDH * 128), BF16)
        bdlos = sbuf(nc, ph, f"{tag}_bdlo", (128, 1), F32)
        bbcs = sbuf(nc, ph, f"{tag}_bbc", (DSTATE, 2), F32)
        bdts = sbuf(nc, ph, f"{tag}_bdt", (128, NDH), F32)
        dlo = sbuf(nc, ph, f"{tag}_dlo", (128, TCA), BF16)
        dch = sbuf(nc, ph, f"{tag}_dch", (128, TCA), F32)
        bful = sbuf(nc, ph, f"{tag}_bful", (DSTATE, L), F32)
        cful = sbuf(nc, ph, f"{tag}_cful", (DSTATE, L), F32)
        t1 = sbuf(nc, ph, f"{tag}_t1", (128, TCA), F32)
        t2 = sbuf(nc, ph, f"{tag}_t2", (128, TCA), F32)
        dout = sbuf(nc, ph, f"{tag}_dout", (128, TCA), BF16)

        SL = ln_sems(b, tag)
        lw = b.sem(f"{tag}_lw")
        mmd = b.sem(f"{tag}_mmd")
        dsc = b.sem(f"{tag}_dsc")
        dvc = b.sem(f"{tag}_dvc")
        std = b.sem(f"{tag}_std")
        stg = b.sem(f"{tag}_stg")

        b.dma("gpsimd", wdl[:].rearrange("p (dt f) -> p dt f", dt=NDT),
              wdloT.rearrange("(dt p) f -> p dt f", p=128), lw)
        b.dma("gpsimd", wbc[:].rearrange("p (dt f) -> p dt f", dt=NDT)[:, :, 0:DSTATE],
              wbT.rearrange("(dt p) f -> p dt f", p=128), lw)
        b.dma("gpsimd", wbc[:].rearrange("p (dt f) -> p dt f", dt=NDT)[:, :, DSTATE:2 * DSTATE],
              wcT.rearrange("(dt p) f -> p dt f", p=128), lw)
        b.dma("gpsimd", wdt[:], wdtT, lw)
        b.dma("sync", bdlos[:], bdlo, lc)
        b.dma("sync", bbcs[:], bbc, lc)
        b.dma("sync", bdts[:], bdt, lc)
        b.barrier([lw, lc])
        wbc_r = wbc[:].rearrange("p (dt f) -> p dt f", dt=NDT)
        wdl_r = wdl[:].rearrange("p (dt f) -> p dt f", dt=NDT)

        with contextlib.ExitStack() as ps:
            ptr = [psum(nc, ps, f"{tag}_ptr{i}", (128, 1024), BF16) for i in range(2)]
            pbc = psum(nc, ps, f"{tag}_pbc", (DSTATE, TCA), F32)
            pcc = psum(nc, ps, f"{tag}_pcc", (DSTATE, TCA), F32)
            pdl = psum(nc, ps, f"{tag}_pdl", (128, TCA), F32)
            pdd = [psum(nc, ps, f"{tag}_pdd{i}", (128, TCA), F32) for i in range(2)]
            for c in range(L // TCA):
                l0 = c * TCA
                b.barrier([mmd, dsc, dvc, std, stg])
                emit_ln(b, ctx, xin[l0:l0 + TCA, :], xnf, TCA, SL, pool, ptr)
                b.barrier([SL["pev"], SL["xnd"], SL["sqd"]])
                for k in range(NDH):
                    dt = dh_off + k
                    b.dma("sync", xnh_d[k * 128:(k + 1) * 128, l0:l0 + TCA],
                          xnf[:, dt * TCA:(dt + 1) * TCA], std)
                for dt in range(NDT):
                    b.do("tensor", lambda e, d=dt: e.matmul(
                        pbc[:], wbc_r[:, d, 0:DSTATE], xnf[:, d * TCA:d * TCA + TCA],
                        start=(d == 0), stop=(d == NDT - 1)))
                for dt in range(NDT):
                    b.do("tensor", lambda e, d=dt: e.matmul(
                        pcc[:], wbc_r[:, d, DSTATE:2 * DSTATE], xnf[:, d * TCA:d * TCA + TCA],
                        start=(d == 0), stop=(d == NDT - 1)),
                        inc=(mmd if dt == NDT - 1 else None))
                b.wait("vector", mmd, None)
                b.do("vector", lambda e, z=l0: e.tensor_scalar(
                    bful[:, z:z + TCA], pbc[:], bbcs[:, 0:1], None, ALU.add))
                b.do("vector", lambda e, z=l0: e.tensor_scalar(
                    cful[:, z:z + TCA], pcc[:], bbcs[:, 1:2], None, ALU.add), inc=dvc)
                for dt in range(NDT):
                    b.do("tensor", lambda e, d=dt: e.matmul(
                        pdl[:], wdl_r[:, d, :], xnf[:, d * TCA:d * TCA + TCA],
                        start=(d == 0), stop=(d == NDT - 1)),
                        inc=(mmd if dt == NDT - 1 else None))
                b.wait("vector", mmd, None)
                b.do("vector", lambda e: e.tensor_scalar(
                    dlo[:], pdl[:], bdlos[:, 0:1], None, ALU.add), inc=dvc)
                for mt in range(NDH):
                    P = pdd[mt % 2]
                    if mt >= 2:
                        b.wait("tensor", stg, None)
                        b.wait("tensor", dvc, None)
                    b.wait("tensor", dvc, None)
                    b.do("tensor", lambda e, Pp=P, m=mt: e.matmul(
                        Pp[:], wdt[:, m * 128:(m + 1) * 128], dlo[:],
                        start=True, stop=True), inc=mmd)
                    b.wait("vector", mmd, None)
                    b.do("vector", lambda e, Pp=P, m=mt: e.tensor_scalar(
                        dch[:], Pp[:], bdts[:, m:m + 1], None, ALU.add))
                    b.do("vector", lambda e: e.tensor_scalar(t1[:], dch[:], -1.0, None, ALU.mult))
                    b.do("vector", lambda e: e.tensor_tensor(t1[:], dch[:], t1[:], ALU.min))
                    b.do("vector", lambda e: e.tensor_scalar(t2[:], dch[:], 0.0, None, ALU.max), inc=dvc)
                    b.wait("scalar", dvc, None)
                    b.do("scalar", lambda e: e.activation(t1[:], t1[:], AF.Exp), inc=dsc)
                    b.wait("vector", dsc, None)
                    b.do("vector", lambda e: e.tensor_scalar(t1[:], t1[:], 1.0, None, ALU.add), inc=dvc)
                    b.wait("scalar", dvc, None)
                    b.do("scalar", lambda e: e.activation(t1[:], t1[:], AF.Ln), inc=dsc)
                    b.wait("vector", dsc, None)
                    b.wait("vector", stg, None)
                    b.do("vector", lambda e: e.tensor_tensor(dout[:], t2[:], t1[:], ALU.add), inc=dvc)
                    b.wait("gpsimd", dvc, None)
                    b.dma("gpsimd", delta_d[mt * 128:(mt + 1) * 128, l0:l0 + TCA], dout[:], stg)
            b.wait("sync", dvc, None)
            b.dma("sync", bc_d[0:DSTATE, :], bful[:], std)
            b.dma("sync", bc_d[DSTATE:2 * DSTATE, :], cful[:], std)
            b.barrier([std, stg, mmd, dsc, dvc])

    with contextlib.ExitStack() as ph:
        dblk = sbuf(nc, ph, f"{tag}_dblk", (128, 2 * L), BF16)
        xblk = sbuf(nc, ph, f"{tag}_xblk", (128, 2 * L), BF16)
        dxb = sbuf(nc, ph, f"{tag}_dx", (128, 2 * L), BF16)
        ybl = sbuf(nc, ph, f"{tag}_ybl", (128, 2 * L), F32)
        bb_ = [sbuf(nc, ph, f"{tag}_bb{i}", (128, L), BF16) for i in range(2)]
        cb_ = [sbuf(nc, ph, f"{tag}_cb{i}", (128, L), BF16) for i in range(2)]
        ab_ = [sbuf(nc, ph, f"{tag}_ab{i}", (128, L), F32) for i in range(2)]
        ub_ = [sbuf(nc, ph, f"{tag}_ub{i}", (128, L), BF16) for i in range(2)]
        hb_ = sbuf(nc, ph, f"{tag}_hb", (128, L), BF16)
        tb_ = sbuf(nc, ph, f"{tag}_tb", (128, L), BF16)
        alg = sbuf(nc, ph, f"{tag}_alg", (128, NDH * DSTATE), F32)
        dds = sbuf(nc, ph, f"{tag}_dds", (128, NDH), F32)
        zcol = sbuf(nc, ph, f"{tag}_zcol", (128, 1), F32)
        xrt = sbuf(nc, ph, f"{tag}_xrt", (128, 2 * 256), F32)
        xot = sbuf(nc, ph, f"{tag}_xot", (128, 2 * 256), F32)

        la = b.sem(f"{tag}_la")
        lbk = b.sem(f"{tag}_lbk")
        lbc = [b.sem(f"{tag}_lbc0"), b.sem(f"{tag}_lbc1")]
        sca = b.sem(f"{tag}_sca")
        gpu = b.sem(f"{tag}_gpu")
        vsc = b.sem(f"{tag}_vsc")
        ptz = b.sem(f"{tag}_ptz")
        aev = b.sem(f"{tag}_aev")
        lrr = [b.sem(f"{tag}_lrr0"), b.sem(f"{tag}_lrr1")]
        stt = [b.sem(f"{tag}_stt0"), b.sem(f"{tag}_stt1")]

        b.dma("sync", alg[:], alog, la)
        b.dma("sync", dds[:], dvec, la)
        b.do("vector", lambda e: e.memset(zcol[:], 0.0))
        b.wait("scalar", la, None)
        b.do("scalar", lambda e: e.activation(alg[:], alg[:], AF.Exp), inc=sca)
        b.wait("vector", sca, None)
        b.do("vector", lambda e: e.tensor_scalar(alg[:], alg[:], -1.0, None, ALU.mult), inc=vsc)
        alg_r = alg[:].rearrange("p (k s) -> p k s", k=NDH)

        with contextlib.ExitStack() as ps:
            ptf = [psum(nc, ps, f"{tag}_ptf{i}", (128, 512), F32) for i in range(2)]
            st_snap = [0, 0]
            nplane = 0
            ntp = 0
            for blk in range(NDH // 2):
                b.barrier([vsc, gpu, sca, aev], engines=("sync",))
                b.dma("sync", dblk[:].rearrange("p (k l) -> p k l", k=2),
                      delta_d.rearrange("(k p) l -> p k l", p=128)[:, blk * 2:blk * 2 + 2, :], lbk)
                b.dma("sync", xblk[:].rearrange("p (k l) -> p k l", k=2),
                      xnh_d.rearrange("(k p) l -> p k l", p=128)[:, blk * 2:blk * 2 + 2, :], lbk)
                b.wait("vector", lbk, None)
                for k in range(2):
                    b.do("vector", lambda e, kk=k: e.tensor_tensor(
                        dxb[:, kk * L:(kk + 1) * L], dblk[:, kk * L:(kk + 1) * L],
                        xblk[:, kk * L:(kk + 1) * L], ALU.mult))
                    b.do("vector", lambda e, kk=k, d2=blk * 2 + k: e.tensor_scalar(
                        ybl[:, kk * L:(kk + 1) * L], xblk[:, kk * L:(kk + 1) * L],
                        dds[:, d2:d2 + 1], None, ALU.mult), inc=vsc)
                for s in range(DSTATE):
                    sp = s % 2
                    b.wait("gpsimd", gpu, None)
                    b.wait("gpsimd", vsc, None)
                    b.dma("gpsimd", bb_[sp][:], bc_d[s:s + 1, :].partition_broadcast(128), lbc[sp])
                    b.dma("gpsimd", cb_[sp][:], bc_d[DSTATE + s:DSTATE + s + 1, :].partition_broadcast(128), lbc[sp])
                    for k in range(2):
                        ap_ = ab_[nplane % 2]
                        up_ = ub_[nplane % 2]
                        b.wait("scalar", vsc, max(0, vsc.n - 1))
                        b.do("scalar", lambda e, kk=k, d2=blk * 2 + k, ss=s, aa=ap_: e.activation(
                            aa[:], dblk[:, kk * L:(kk + 1) * L], AF.Exp,
                            scale=alg_r[:, d2, ss:ss + 1]), inc=sca)
                        b.wait("gpsimd", lbc[sp], None)
                        b.wait("gpsimd", vsc, None)
                        b.do("gpsimd", lambda e, kk=k, uu=up_, bbx=bb_[sp]: e.tensor_tensor(
                            uu[:], dxb[:, kk * L:(kk + 1) * L], bbx[:], ALU.mult), inc=gpu)
                        b.wait("vector", sca, None)
                        b.wait("vector", gpu, None)
                        b.do("vector", lambda e, aa=ap_, uu=up_: e.tensor_tensor_scan(
                            hb_[:], aa[:], uu[:], zcol[:], ALU.mult, ALU.add))
                        b.do("vector", lambda e, cbx=cb_[sp]: e.tensor_tensor(
                            tb_[:], hb_[:], cbx[:], ALU.mult))
                        b.do("vector", lambda e, kk=k: e.tensor_tensor(
                            ybl[:, kk * L:(kk + 1) * L], tb_[:], ybl[:, kk * L:(kk + 1) * L],
                            ALU.add), inc=vsc)
                        nplane += 1
                for ti in range(L // 128):
                    p = ti % 2
                    bank = ptf[ntp % 2]
                    if ntp >= 2:
                        b.wait("tensor", aev, ntp - 1)
                    b.wait("tensor", vsc, None)
                    for k in range(2):
                        b.do("tensor", lambda e, bk=bank, kk=k, t=ti: e.transpose(
                            bk[:, kk * 128:(kk + 1) * 128],
                            ybl[:, kk * L + t * 128: kk * L + t * 128 + 128],
                            idnf[:]), inc=(ptz if k == 1 else None))
                    b.wait("sync", aev, max(0, aev.n - 1))
                    b.dma("sync", xrt[:, (p) * 256:(p + 1) * 256],
                          xres[ti * 128:(ti + 1) * 128, blk * 256:(blk + 1) * 256], lrr[p])
                    b.wait("vector", ptz, None)
                    b.wait("vector", lrr[p], None)
                    b.wait("vector", stt[p], st_snap[p])
                    b.do("vector", lambda e, bk=bank, pp=p: e.tensor_tensor(
                        xot[:, pp * 256:(pp + 1) * 256], bk[:, 0:256], xrt[:, pp * 256:(pp + 1) * 256],
                        ALU.add), inc=aev)
                    b.wait("sync", aev, None)
                    b.dma("sync", yout[ti * 128:(ti + 1) * 128, blk * 256:(blk + 1) * 256],
                          xot[:, p * 256:(p + 1) * 256], stt[p])
                    st_snap[p] = stt[p].n
                    ntp += 1
        b.barrier([stt[0], stt[1]])


_CACHE = {}


def _progs():
    if "l0f" not in _CACHE:
        _CACHE["l0f"] = build_l0ffn()
        _CACHE["l0a"] = build_l0attn()
        _CACHE["l1f"] = build_l1ffn()
        _CACHE["l1a"] = build_l1attn()
    return _CACHE


def _rope_tables(L=4096):
    f32 = np.float32
    inv = 1.0 / 10000.0 ** (np.arange(0, 128, 2, dtype=f32) / 128)
    t = np.arange(L, dtype=f32)
    f = np.outer(t, inv)
    f = np.repeat(f, 2, axis=-1)
    Rc, Rs = np.cos(f), np.sin(f)
    Rsm = Rs.copy()
    Rsm[:, 0::2] = -Rs[:, 0::2]
    return (np.ascontiguousarray(Rc.T).astype(BF16NP),
            np.ascontiguousarray(Rsm.T).astype(BF16NP))


def _swap_pairs(W):
    Ws = W.copy()
    Ws[:, 0::2] = W[:, 1::2]
    Ws[:, 1::2] = W[:, 0::2]
    return Ws


def _run_l0ffn(x8, wgT, wuT, wdT, sgp, sup, sdp, idn):
    nc = _progs()["l0f"]
    im = [dict(x=np.ascontiguousarray(x8[c]), wgT=wgT, wuT=wuT, wdT=wdT,
               sg=sgp, su=sup, sd=sdp, idn=idn) for c in range(8)]
    res = run_bass_kernel_spmd(nc, im, core_ids=list(range(8)))
    return [np.asarray(res.results[c]["y"]) for c in range(8)]


def kernel(L0_attn_x, L0_attn_W_dbc, L0_attn_b_dbc, L0_attn_W_dt, L0_attn_b_dt,
           L0_attn_A_log, L0_attn_D, L0_ffn_wq_gate, L0_ffn_s_gate, L0_ffn_wq_up,
           L0_ffn_s_up, L0_ffn_wq_down, L0_ffn_s_down, L1_attn_Wq, L1_attn_Wk,
           L1_attn_Wv, L1_ffn_gate_w, L1_ffn_gate_b, L1_ffn_w1, L1_ffn_w2,
           L1_ffn_w3, L1_ffn_sw1, L1_ffn_sw2, L1_ffn_sw3):
    f32 = np.float32
    x = np.asarray(L0_attn_x, f32)
    B, L, _ = x.shape
    idn = np.eye(128, dtype=BF16NP)
    idnf = np.eye(128, dtype=f32)

    wgT = np.ascontiguousarray(L0_ffn_wq_gate.T)
    wuT = np.ascontiguousarray(L0_ffn_wq_up.T)
    wdT = np.ascontiguousarray(L0_ffn_wq_down.T)
    sgp = np.ascontiguousarray(L0_ffn_s_gate.reshape(44, 128).T.astype(f32))
    sup = np.ascontiguousarray(L0_ffn_s_up.reshape(44, 128).T.astype(f32))
    sdp = np.ascontiguousarray(L0_ffn_s_down.reshape(16, 128).T.astype(f32))

    Wdbc, bdbc = L0_attn_W_dbc, L0_attn_b_dbc
    Wdt, bdt_f = L0_attn_W_dt, L0_attn_b_dt
    Alog, Dv = L0_attn_A_log, L0_attn_D

    w1f = np.concatenate([L1_ffn_w1, L1_ffn_sw1[None]], 0)
    w3f = np.concatenate([L1_ffn_w3, L1_ffn_sw3[None]], 0)
    w2f = np.concatenate([L1_ffn_w2, L1_ffn_sw2[None]], 0)
    w1t = np.ascontiguousarray(w1f.transpose(0, 2, 1).reshape(9, 16, 128, 11, 128).transpose(0, 3, 2, 1, 4))
    w3t = np.ascontiguousarray(w3f.transpose(0, 2, 1).reshape(9, 16, 128, 11, 128).transpose(0, 3, 2, 1, 4))
    w2t = np.ascontiguousarray(w2f.transpose(0, 2, 1).reshape(9, 11, 128, 2048))
    gwT = np.ascontiguousarray(L1_ffn_gate_w.T.astype(f32))
    gbr = np.ascontiguousarray(L1_ffn_gate_b.astype(f32)[None])

    rc_fm, rs_fm = _rope_tables(L)
    cmask = np.zeros((4, 128, 512), dtype=BF16NP)
    ar = np.arange(512)
    for k in range(4):
        for j in range(128):
            cmask[k, j, :] = (ar >= j + k * 128).astype(BF16NP)
    onescol = np.ones((128, 1), dtype=BF16NP)

    progs = _progs()

    def tok_shards(xa):
        return [np.ascontiguousarray(xa[c // 2, (c % 2) * 2048:(c % 2 + 1) * 2048, :])
                for c in range(8)]

    def from_tok_shards(sh):
        return np.stack([np.concatenate([sh[2 * b2], sh[2 * b2 + 1]], 0) for b2 in range(4)])

    y = _run_l0ffn(tok_shards(x), wgT, wuT, wdT, sgp, sup, sdp, idn)
    x = from_tok_shards(y)

    im = []
    for c in range(8):
        bb, dh = c // 2, c % 2
        dsl = slice(dh * 1024, (dh + 1) * 1024)
        osl = slice((1 - dh) * 1024, (2 - dh) * 1024)
        perm = np.r_[np.arange(dsl.start, dsl.stop), np.arange(osl.start, osl.stop)]
        im.append(dict(
            x=np.ascontiguousarray(x[bb][:, perm]),
            xres=np.ascontiguousarray(x[bb][:, dsl]),
            wdloT=np.ascontiguousarray(Wdbc[0:128].T[perm].astype(f32)),
            wbT=np.ascontiguousarray(Wdbc[128:144].T[perm].astype(f32)),
            wcT=np.ascontiguousarray(Wdbc[144:160].T[perm].astype(f32)),
            bdlo=np.ascontiguousarray(bdbc[0:128].reshape(128, 1).astype(f32)),
            bbc=np.ascontiguousarray(np.stack([bdbc[128:144], bdbc[144:160]], axis=1).astype(f32)),
            wdtT=np.ascontiguousarray(Wdt.T[:, dsl].astype(f32)),
            bdt=np.ascontiguousarray(bdt_f[dsl].reshape(8, 128).T.astype(f32)),
            alog=np.ascontiguousarray(Alog[dsl].reshape(8, 128, 16).transpose(1, 0, 2).reshape(128, 128).astype(f32)),
            dvec=np.ascontiguousarray(Dv[dsl].reshape(8, 128).T.astype(f32)),
            idn=idn, idnf=idnf))
    res = run_bass_kernel_spmd(progs["l0a"], im, core_ids=list(range(8)))
    x = np.stack([np.concatenate([np.asarray(res.results[2 * b2]["y"]),
                                  np.asarray(res.results[2 * b2 + 1]["y"])], 1)
                  for b2 in range(4)])

    y = _run_l0ffn(tok_shards(x), wgT, wuT, wdT, sgp, sup, sdp, idn)
    x = from_tok_shards(y)

    base = dict(w1t=w1t, w3t=w3t, w2t=w2t, gwT=gwT, gb=gbr, idn=idn, idnf=idnf,
                onesb=np.ones((1, 128), dtype=BF16NP))
    im = [dict(base, x=s) for s in tok_shards(x)]
    res = run_bass_kernel_spmd(progs["l1f"], im, core_ids=list(range(8)))
    x = from_tok_shards([np.asarray(res.results[c]["y"]) for c in range(8)])

    im = []
    for c in range(8):
        bb, hh = c // 2, c % 2
        wq_sl = np.ascontiguousarray(L1_attn_Wq[:, hh * 1024:(hh + 1) * 1024].astype(f32))
        im.append(dict(
            x=np.ascontiguousarray(x[bb]),
            xres=np.ascontiguousarray(x[bb][:, hh * 1024:(hh + 1) * 1024]),
            wq=wq_sl, wqs=_swap_pairs(wq_sl),
            wk=np.ascontiguousarray(L1_attn_Wk.astype(f32)), wks=_swap_pairs(np.asarray(L1_attn_Wk, f32)),
            wv=np.ascontiguousarray(L1_attn_Wv.astype(f32)),
            rc=rc_fm, rs=rs_fm, cmask=cmask, onescol=onescol, idn=idn))
    res = run_bass_kernel_spmd(progs["l1a"], im, core_ids=list(range(8)))
    x = np.stack([np.concatenate([np.asarray(res.results[2 * b2]["y"]),
                                  np.asarray(res.results[2 * b2 + 1]["y"])], 1)
                  for b2 in range(4)])

    im = [dict(base, x=s) for s in tok_shards(x)]
    res = run_bass_kernel_spmd(progs["l1f"], im, core_ids=list(range(8)))
    x = from_tok_shards([np.asarray(res.results[c]["y"]) for c in range(8)])

    return x.astype(np.float32)
